# version 02 (final)
# baseline (speedup 1.0000x reference)
import numpy as np
import contextlib
import concourse.bass as bass
import concourse.mybir as mybir
from concourse.bass_utils import run_bass_kernel_spmd

F32 = mybir.dt.float32
BF = mybir.dt.bfloat16
AF = mybir.ActivationFunctionType
OP = mybir.AluOpType
AX = mybir.AxisListType

B = 4
T = 2048
D = 1024
H = 16
DH = 64
C = 128
NCH = T // C
FF = 4096
NE = 8
FTL = D // 128
NT = T // 512
ROPE_BASE = 10000.0
EPS_LN = 1e-5
EPS_QK = 1e-6
EPS_ON = 1e-6


def host_constants():
    cst = {}
    cst["IDENT"] = np.eye(128, dtype=np.float32)
    cst["ONESM"] = np.ones((128, 128), np.float32)
    p = np.arange(128)[:, None]
    f = np.arange(128)[None, :]
    cst["MASKA"] = np.where(f >= p, 1e30, 0.0).astype(np.float32)
    cst["MASKB"] = np.where(p >= f, -1e30, 0.0).astype(np.float32)
    blk = (p // 64) != (f // 64)
    cst["MASKC"] = np.where((f >= p) | blk, 1e30, 0.0).astype(np.float32)
    cst["MASKD"] = np.where((p >= f) | blk, -1e30, 0.0).astype(np.float32)
    selA = np.zeros((128, 128), np.float32)
    selB = np.zeros((128, 128), np.float32)
    for i in range(128):
        h64, r = i // 64, i % 64
        selA[h64 * 64 + 2 * (r % 32), i] = 1.0
        selB[h64 * 64 + 2 * (r % 32) + 1, i] = 1.0
    cst["PERMA"] = selA
    cst["PERMB"] = selB
    t = np.arange(T, dtype=np.float64)
    inv_freq = 1.0 / ROPE_BASE ** (np.arange(0, DH, 2, dtype=np.float64) / DH)
    Cr = np.zeros((128, T), np.float64)
    Sr = np.zeros((128, T), np.float64)
    for r in range(64):
        fr = inv_freq[2 * (r % 16)]
        ang = t * fr
        if r < 32:
            Cr[r] = np.cos(ang)
            Sr[r] = -np.sin(ang)
        else:
            Cr[r] = np.sin(ang)
            Sr[r] = np.cos(ang)
    Cr[64:] = Cr[:64]
    Sr[64:] = Sr[:64]
    cst["CROPE"] = Cr.astype(np.float32)
    cst["SROPE"] = Sr.astype(np.float32)
    rst = np.ones((16, T), np.float32)
    rst[:, ::C] = 0.0
    cst["RST"] = rst
    cst["CBIAS"] = np.tile(np.array([[EPS_LN, 1.0, EPS_LN * EPS_LN, EPS_QK, EPS_ON]], np.float32), (128, 1))
    bd = np.zeros((128, 2), np.float32)
    bd[:64, 0] = 1.0
    bd[64:, 1] = 1.0
    cst["BD64"] = bd
    cst["BD64T"] = np.ascontiguousarray(bd.T)
    return cst


class Prog:
    def __init__(self, sems):
        self.order = []
        self.sems = sems
        self.cnt = {k: 0 for k in sems}

    def op(self, eng, fn, inc=None, inc_n=1):
        if inc is not None:
            sem = self.sems[inc]
            self.cnt[inc] += inc_n
            def g(e, fn=fn, sem=sem, n=inc_n):
                fn(e).then_inc(sem, n)
            self.order.append((eng, g))
        else:
            self.order.append((eng, fn))
        return self.cnt[inc] if inc else None

    def dma(self, eng, out, in_, inc):
        sem = self.sems[inc]
        self.cnt[inc] += 16
        def g(e, out=out, in_=in_, sem=sem):
            e.dma_start(out=out, in_=in_).then_inc(sem, 16)
        self.order.append((eng, g))
        return self.cnt[inc]

    def val(self, sem):
        return self.cnt[sem]

    def fence(self, eng):
        sem = {"vector": "vs", "scalar": "ss"}[eng]
        outap, inap = self.fence_aps[eng]
        s = self.sems[sem]
        self.cnt[sem] += 1
        if eng == "vector":
            def g(e, s=s, outap=outap, inap=inap):
                e.tensor_copy(out=outap, in_=inap).then_inc(s, 1)
        else:
            def g(e, s=s, outap=outap, inap=inap):
                e.copy(out=outap, in_=inap).then_inc(s, 1)
        self.order.append((eng, g))
        return self.cnt[sem]

    def waitf(self, eng, sem, val=None):
        v = self.cnt[sem] if val is None else val
        if sem == "vs" and eng != "vector" and v > 0 and v == self.cnt[sem]:
            v = self.fence("vector")
        elif sem == "ss" and eng != "scalar" and v > 0 and v == self.cnt[sem]:
            v = self.fence("scalar")
        self.wait(eng, sem, v)

    def selfw(self, eng):
        sem = {"vector": "vs", "scalar": "ss", "tensor": "ts"}[eng]
        self.wait(eng, sem)

    def wait(self, eng, sem, val=None):
        v = self.cnt[sem] if val is None else val
        if v <= 0:
            return
        s = self.sems[sem]
        def g(e, s=s, v=v):
            e.wait_ge(s, v)
        self.order.append((eng, g))

    def emit(self, block):
        order = self.order
        def mk(name):
            def body(eng):
                for e, fn in order:
                    if e == name:
                        fn(eng)
            return body
        block.sync(mk("sync"))
        block.gpsimd(mk("gpsimd"))
        block.scalar(mk("scalar"))
        block.vector(mk("vector"))
        block.tensor(mk("tensor"))


def full_barrier(p):
    for eng in ("sync", "vector", "scalar", "tensor", "gpsimd"):
        for sem in ("ts", "vs", "ss", "ds", "dc", "dg"):
            p.wait(eng, sem)


class Ctx:
    pass


def build(nB=B, taps=(), dbg=False, stop_after=None):
    nc = bass.Bass("TRN2", target_bir_lowering=False)
    cst = host_constants()
    cx = Ctx()
    cx.nc = nc
    cx.taps = set(taps)
    cx.TAP = {}

    x_in = nc.dram_tensor("x", [nB, T, D], F32, kind="ExternalInput").ap()
    WN = {}
    for nm, shp in [
        ("aWq", [D, D]), ("aWk", [D, D]), ("aWv", [D, D]), ("aWg", [D, D]), ("aWo", [D, D]),
        ("abw", [D, H]), ("agkw", [D, H]),
        ("fWq", [D, D]), ("fWk", [D, D]), ("fWv", [D, D]), ("fWo", [D, D]), ("fWg", [D, NE]),
        ("fW1", [NE, D, FF]), ("fW2", [NE, FF, D]),
    ]:
        WN[nm] = nc.dram_tensor(nm, shp, F32, kind="ExternalInput").ap()
    SVD = {}
    for nm, shp in [
        ("abb", [16, 1]), ("agkb", [16, 1]), ("aAlog", [16, 1]), ("adtb", [16, 1]),
        ("aD", [1, 16]), ("aonw", [128, 1]),
        ("aqcw", [128, FTL, 4]), ("aqcb", [128, FTL]),
        ("akcw", [128, FTL, 4]), ("akcb", [128, FTL]),
        ("avcw", [128, FTL, 4]), ("avcb", [128, FTL]),
        ("fln1g", [128, FTL]), ("fln1b", [128, FTL]),
        ("fln2g", [128, FTL]), ("fln2b", [128, FTL]),
        ("fb1", [128, NE, FF // 128]), ("fb2", [128, NE, FTL]),
    ]:
        SVD[nm] = nc.dram_tensor(nm, shp, F32, kind="ExternalInput").ap()
    CTD = {nm: nc.dram_tensor("c_" + nm, list(a.shape), F32, kind="ExternalInput").ap()
           for nm, a in cst.items()}
    out = nc.dram_tensor("out", [nB, T, D], F32, kind="ExternalOutput").ap()
    outdummy = nc.dram_tensor("outdummy", [nB, T, D], F32, kind="Internal").ap()
    outstage = nc.dram_tensor("outstage", [nB, T, D], F32, kind="Internal").ap()

    SC = {}
    sckind = "ExternalOutput" if dbg else "Internal"
    for nm, shp, dt in [
        ("X0T", [nB, D, T], F32), ("QT", [nB, D, T], BF), ("KT", [nB, D, T], BF),
        ("VT", [nB, D, T], BF), ("GT", [nB, D, T], BF), ("OGT", [nB, D, T], BF),
        ("H2", [nB, D, T], F32), ("RP", [nB, D, T], F32), ("HH", [nB, D, T], BF),
        ("MT", [nB, D, T], BF), ("RR", [nB, D, T], F32),
    ]:
        SC[nm] = nc.dram_tensor("s_" + nm, shp, dt, kind=sckind).ap()
    WB = {}
    for nm in ["aWq", "aWk", "aWv", "aWg", "aWo", "abw", "agkw",
               "fWq", "fWk", "fWv", "fWo", "fWg", "fW1", "fW2"]:
        WB[nm] = nc.dram_tensor("wb_" + nm, list(WN[nm].shape), BF, kind="Internal").ap()

    for nm, shp, dt in [("rope0", [128, 512], F32), ("bcast0", [128, 512], F32),
                        ("rstd0", [2, 512], F32), ("sqb0", [128, 512], BF),
                        ("psum0", [2, 512], F32),
                        ("cNt", [128, 128], BF), ("cMTc", [128, 128], BF), ("cX", [128, 128], BF),
                        ("cRc", [128, 128], BF), ("cQSc", [128, 128], F32), ("cBQK", [128, 128], F32),
                        ("cMAT1", [64, 128], BF), ("cMAT2T", [64, 128], F32),
                        ("cATm", [64, 64], F32), ("cBCm", [64, 64], F32),
                        ("cKTOK", [128, 16 * 64], BF), ("cVTOK", [128, 16 * 64], BF),
                        ("cgT", [128, 256], F32), ("cgbT", [128, 256], F32), ("cbT", [128, 256], F32),
                        ("clamT", [128, 256], F32), ("cbPT", [128, 256], F32), ("cLAMH", [128, 256], F32),
                        ("cROW0G", [1, T], F32), ("cROW0EG", [1, T], F32), ("cOH", [64, T], F32),
                        ("x1g0", [128, FTL * 512], F32), ("ogl0", [128, FTL * 512], BF),
                        ("ps0", [128, 512], F32), ("x0l0", [128, FTL * 512], F32),
                        ("ws00", [128, FTL * 1024], BF),
                        ("yH2S", [128, 512], F32), ("yMBF", [128, 512], BF),
                        ("yMR", [1, 512], F32), ("yRS", [1, 512], F32), ("yVR", [1, 512], F32),
                        ("yX1G", [128, FTL * 512], F32), ("yLNC", [128, 16 * 512], BF)]:
        if nm in cx.taps or (nm.startswith("c") and "chk" in cx.taps) or (nm in ("x1g0","ogl0","ps0","x0l0","ws00") and "p35" in cx.taps) or (nm.startswith("y") and "p4y" in cx.taps):
            cx.TAP[nm] = nc.dram_tensor("t_" + nm, shp, dt, kind="ExternalOutput").ap()

    def tapf(nm, shp, dt=F32):
        if nm in cx.taps:
            cx.TAP[nm] = nc.dram_tensor("t_" + nm, shp, dt, kind="ExternalOutput").ap()
            return cx.TAP[nm]
        return None
    cx.tapf = tapf

    with contextlib.ExitStack() as es:
        ACOLS = 96 * 1024
        ARN = es.enter_context(nc.sbuf_tensor("ARN", [128, ACOLS], BF))

        def vw(off, shape, dt):
            esz = 2 if dt == BF else 4
            n = 1
            for s in shape[1:]:
                n *= s
            assert off % 64 == 0, off
            base = ARN[:, off // 2: off // 2 + (n * esz) // 2]
            if dt != BF:
                base = base.bitcast(dt)
            if len(shape) == 3:
                base = base.rearrange("p (a b) -> p a b", b=shape[2])
            if shape[0] != 128:
                base = base[0:shape[0]]
            return base

        CO = {}
        o = 0
        def place(nm, shape, dt):
            nonlocal o
            esz = 2 if dt == BF else 4
            n = esz
            for s in shape[1:]:
                n *= s
            CO[nm] = (o, shape, dt)
            o += (n + 63) // 64 * 64
        place("IDBF", [128, 128], BF)
        place("IDF", [128, 128], F32)
        place("ONESB", [128, 128], BF)
        place("ONESF", [128, 128], F32)
        place("MASKA", [128, 128], F32)
        place("MASKB", [128, 128], F32)
        place("MASKC", [128, 128], F32)
        place("MASKD", [128, 128], F32)
        place("PERMA", [128, 128], BF)
        place("PERMB", [128, 128], BF)
        place("BD64", [128, 2], BF)
        place("BD64T", [2, 128], BF)
        place("CROPE", [128, T], F32)
        place("SROPE", [128, T], F32)
        place("RST", [16, T], F32)
        place("S1024", [128, 1], BF)
        place("CBIAS", [128, 5], F32)
        for nm in SVD:
            place(nm, list(SVD[nm].shape), F32)
        place("negeA", [16, 1], F32)
        place("gkbias", [16, 1], F32)
        place("Pg", [16, T], F32)
        place("Pgb", [16, T], F32)
        place("Pgp", [16, T], F32)
        place("Pegp", [16, T], F32)
        for nm in ["gT", "gbT", "gCT", "bT", "lamT", "bPT", "LAMH"]:
            place(nm, [128, NCH, 16], F32)
        place("STATS", [128, 272], F32)
        PERM_END = o
        assert PERM_END <= 72 * 1024, PERM_END
        SBase = 72 * 1024
        def placeb(nm, off, shape, dt):
            CO[nm] = (SBase + off, shape, dt)
        placeb("HREG", 0, [128, FTL, T], BF)
        placeb("WS0", 32 * 1024, [128, FTL, 1024], BF)
        placeb("WS1", 48 * 1024, [128, FTL, 1024], BF)
        placeb("CONVP", 64 * 1024, [128, T + 3], F32)
        placeb("CONVO", 73 * 1024, [128, T], F32)
        placeb("CONVQ", 81 * 1024, [128, T], F32)
        placeb("RB", 89 * 1024, [128, T], BF)
        placeb("ROPE1", 93 * 1024, [128, 512], F32)
        placeb("ROPE2", 95 * 1024, [128, 512], F32)
        placeb("ROPEO", 97 * 1024, [128, 512], F32)
        placeb("SQB", 99 * 1024, [128, 512], BF)
        placeb("RSTD2", 100 * 1024, [2, 512], F32)
        CO["RSTD2B"] = (63 * 1024, [2, 512], BF)
        placeb("STG", 102 * 1024, [128, NT, 512], BF)
        placeb("X0TM0", 97 * 1024, [128, D], F32)
        placeb("X0TM1", 101 * 1024, [128, D], F32)
        placeb("HTM", 105 * 1024, [128, D], F32)
        placeb("HB", 109 * 1024, [128, D], BF)
        placeb("X0S", 111 * 1024, [128, FTL, 128], F32)
        placeb("GSTG", 93 * 1024, [128, T], BF)
        placeb("RWbeta", 64 * 1024, [16, T], F32)
        placeb("RWz", 72 * 1024, [16, T], F32)
        placeb("RWt", 80 * 1024, [16, T], F32)
        placeb("RWglog", 88 * 1024, [16, T], F32)
        placeb("RWlnb", 96 * 1024, [16, T], F32)
        placeb("RWc", 104 * 1024, [16, T], F32)
        placeb("QTL", 0, [64, T], BF)
        placeb("KTL", 4 * 1024, [64, T], BF)
        placeb("VTL", 8 * 1024, [64, T], BF)
        placeb("ROW0G", 12 * 1024, [1, T], F32)
        placeb("ROW0GB", 20 * 1024, [1, T], F32)
        placeb("ROW0GP", 28 * 1024, [1, T], F32)
        placeb("ROW0EG", 36 * 1024, [1, T], F32)
        placeb("KTOK", 44 * 1024, [128, NCH, DH], BF)
        placeb("VTOK", 46 * 1024, [128, NCH, DH], BF)
        so = 48 * 1024
        for nm in ["Nt", "NTt", "Pa", "PaT", "Pb", "PbT", "Xc", "Xd", "Rc", "MTc", "KPc", "QKbf", "NToff"]:
            placeb(nm, so, [128, 128], BF)
            so += 256
        for nm in ["ARG", "EbL", "QSc", "QKB", "BQK", "LAMI"]:
            placeb(nm, so, [128, 128], F32)
            so += 512
        assert so <= 56 * 1024
        placeb("MAT1", 56 * 1024, [64, NCH * 128], BF)
        placeb("MAT2T", 60 * 1024, [64, NCH * 128], F32)
        placeb("ATm", 68 * 1024, [64, NCH * 64], F32)
        placeb("BCm", 72 * 1024, [64, NCH * 64], F32)
        placeb("OH", 76 * 1024, [64, T], F32)
        placeb("SST0", 84 * 1024, [64, 64], F32)
        placeb("SST1", 85 * 1024, [64, 64], F32)
        placeb("GTld", 86 * 1024, [64, T], BF)
        placeb("ONS", 90 * 1024, [64, 512], F32)
        placeb("ONS2", 92 * 1024, [64, 512], F32)
        placeb("RST1", 94 * 1024, [1, 512], F32)
        CO["RST1B"] = (64 * 1024, [1, 512], BF)
        placeb("OGS", 96 * 1024, [64, T], BF)
        placeb("SQ64", 100 * 1024, [64, 512], BF)
        placeb("OGL", 0, [128, FTL, 512], BF)
        placeb("X1G", 8 * 1024, [128, FTL, 512], F32)
        placeb("X0L", 92 * 1024, [128, FTL, 512], F32)
        placeb("RLD", 56 * 1024, [128, FTL, 512], F32)
        placeb("LNC", 56 * 1024, [128, 16, 512], BF)
        placeb("MR", 72 * 1024, [1, 512], F32)
        placeb("VRr", 74 * 1024, [1, 512], F32)
        placeb("RSr", 76 * 1024, [1, 512], F32)
        placeb("RS2r", 78 * 1024, [1, 512], F32)
        placeb("H2S", 80 * 1024, [128, 512], F32)
        placeb("TS1", 82 * 1024, [128, 512], F32)
        placeb("TS2", 84 * 1024, [128, 512], F32)
        placeb("VH", 86 * 1024, [128, NCH, 68], BF)
        placeb("CTXA", 89 * 1024, [128, 512], F32)
        placeb("RECRF", 92 * 1024, [128, 512], F32)
        placeb("RECBF", 94 * 1024, [128, 512], BF)
        placeb("RC64", 95 * 1024, [64, 512], F32)
        placeb("IQ", 97 * 1024, [64, T], BF)
        placeb("IK", 101 * 1024, [64, T], BF)
        placeb("IV", 105 * 1024, [64, T], BF)
        placeb("CTB", 109 * 1024, [64, 512], BF)
        placeb("MBF", 110 * 1024, [128, 512], BF)
        placeb("WREG", 0, [128, FTL, 4096], BF)
        placeb("WR0Q", 0, [128, 16, 1024], BF)
        placeb("WR1Q", 32 * 1024, [128, 16, 1024], BF)
        placeb("HB1", 64 * 1024, [128, 32, 512], BF)
        placeb("ACC", 96 * 1024, [128, FTL, 512], F32)
        placeb("MW", 112 * 1024, [128, FTL, 512], BF)
        def placea(nm, absoff, shape, dt):
            CO[nm] = (absoff, shape, dt)
        placea("RL5", CO["CROPE"][0], [128, FTL, 512], F32)
        placea("GBCA", CO["Pg"][0], [128, NE, 512], F32)
        ro = CO["Pgp"][0]
        placea("PRB", ro, [128, NE], F32); ro += 64
        placea("GSB", ro, [128, NE], F32); ro += 64
        placea("GSB2", ro, [128, NE], F32); ro += 64
        placea("GCOL", ro, [128, 8], F32); ro += 64
        placea("G4", ro, [128, 4, NE], F32); ro += 256
        placea("GRE", ro, [1, 512], F32); ro += 2048
        placea("TS5", ro, [128, 512], F32); ro += 2048
        placea("WGS", ro, [128, FTL, NE], BF); ro += 128
        assert ro <= CO["Pgp"][0] + 8192, ro
        placea("OST", CO["gT"][0], [128, 1024], F32)
        MAXOFF = SBase + 120 * 1024
        assert MAXOFF <= ACOLS * 2, MAXOFF

        def gv(nm):
            off, shape, dt = CO[nm]
            return vw(off, shape, dt)
        cx.gv = gv

        pb = [es.enter_context(nc.psum_tensor(f"pb{i}", [128, 512], F32)) for i in range(8)]
        semnames = ["ds", "dc", "dg", "dw", "ts", "vs", "ss", "od"]
        sems = {nm: es.enter_context(nc.semaphore(nm)) for nm in semnames}
        p = Prog(sems)
        fen = gv("STATS")
        p.fence_aps = {"vector": (fen[0:1, 16:80], fen[0:1, 80:144]),
                       "scalar": (fen[0:1, 144:208], fen[0:1, 208:272])}
        cx.p = p
        cx.pb = pb
        cx.bank_prod = [0] * 8
        cx.bank_readers = [[] for _ in range(8)]
        cx.SC = SC
        cx.WB = WB
        cx.out = out
        cx.outdummy = outdummy
        cx.outstage = outstage
        cx.x_in = x_in
        cx.ws_rd = {"WS0": (0, 0), "WS1": (0, 0), "WGS": (0, 0)}

        def mm_into(bank, fns, extra_waits=()):
            for s_, v_ in list(cx.bank_readers[bank]) + list(extra_waits):
                if s_ == "vs" and v_ > 0 and v_ == p.cnt[s_]:
                    v_ = p.fence("vector")
                elif s_ == "ss" and v_ > 0 and v_ == p.cnt[s_]:
                    v_ = p.fence("scalar")
                p.wait("tensor", s_, v_)
            cx.bank_readers[bank] = []
            for f in fns[:-1]:
                p.op("tensor", f)
            p.op("tensor", fns[-1], inc="ts")
            cx.bank_prod[bank] = p.val("ts")
        cx.mm_into = mm_into

        def bank_read(eng, banks, fn, inc_sem, extra_waits=()):
            if isinstance(banks, int):
                banks = [banks]
            for b_ in banks:
                p.wait(eng, "ts", cx.bank_prod[b_])
            for s_, v_ in extra_waits:
                p.waitf(eng, s_, v_)
            v = p.op(eng, fn, inc=inc_sem)
            for b_ in banks:
                cx.bank_readers[b_].append((inc_sem, v))
            return v
        cx.bank_read = bank_read

        cb = gv("CBIAS")
        nc.const_aps.register(F32, EPS_LN, cb[:, 0:1])
        nc.const_aps.register(F32, 1.0, cb[:, 1:2])
        nc.const_aps.register(F32, EPS_LN * EPS_LN, cb[:, 2:3])
        nc.const_aps.register(F32, EPS_QK, cb[:, 3:4])
        nc.const_aps.register(F32, EPS_ON, cb[:, 4:5])
        for nm in ["IDF", "MASKA", "MASKB", "MASKC", "MASKD", "CROPE", "SROPE", "RST", "ONESF", "CBIAS"]:
            src = {"IDF": "IDENT", "ONESF": "ONESM"}.get(nm, nm)
            p.dma("sync", gv(nm), CTD[src], "ds")
        for nm in ["IDBF", "ONESB", "PERMA", "PERMB", "BD64", "BD64T"]:
            src = {"IDBF": "IDENT", "ONESB": "ONESM"}.get(nm, nm)
            p.dma("gpsimd", gv(nm), CTD[src], "dg")
        for nm in SVD:
            p.dma("sync", gv(nm), SVD[nm], "ds")
        for nm in ["aWq", "aWk", "aWv", "aWg", "abw", "agkw", "aWo",
                   "fWq", "fWk", "fWv", "fWo", "fWg"]:
            p.dma("gpsimd", WB[nm], WN[nm], "dg")
        cx.small_casts = p.val("dg")
        p.dma("gpsimd", WB["fW1"], WN["fW1"], "dw")
        p.dma("gpsimd", WB["fW2"], WN["fW2"], "dw")
        cx.w12_casts = p.val("dw")
        cx.preload = p.val("ds")

        p.wait("vector", "ds", cx.preload)
        p.wait("vector", "dg", cx.small_casts)
        p.op("vector", lambda e: e.memset(gv("S1024"), 1.0 / 1024.0), inc="vs")
        p.op("vector", lambda e: e.tensor_add(out=gv("gkbias"), in0=gv("agkb"), in1=gv("adtb")), inc="vs")
        p.wait("scalar", "vs")
        p.op("scalar", lambda e: e.activation(out=gv("negeA"), in_=gv("aAlog"), func=AF.Exp), inc="ss")
        p.wait("vector", "ss")
        p.op("vector", lambda e: e.tensor_scalar(out=gv("negeA"), in0=gv("negeA"), scalar1=-1.0, scalar2=None, op0=OP.mult), inc="vs")

        phases = ["p1", "p2", "p3", "p35", "p4"]
        if stop_after in ("p5a", "p5b", "p5c"):
            stop_after = None
        stopi = phases.index(stop_after) if stop_after in phases else 99
        for b in range(nB):
            phase1(cx, b)
            if stopi >= 1:
                phase2(cx, b)
            if stopi >= 2:
                phase3(cx, b)
            if stopi >= 3:
                phase35(cx, b)
            if stopi >= 4:
                phase4(cx, b)
            full_barrier(p)
        if stopi >= 5:
            phase5(cx, nB, skip_out=(stop_after == "p5a"), skip_experts=(stop_after == "p5b"), dummy_out=(stop_after == "p5c"))
            p.wait("sync", "dc")
            for b in range(nB):
                p.dma("sync", out[b], cx.outstage[b], "od")
        p.wait("sync", "dc")
        p.wait("sync", "ds")
        p.wait("sync", "dg")
        p.wait("sync", "dw")
        p.wait("sync", "vs")
        p.wait("sync", "ss")
        p.wait("sync", "ts")

        p.wait("sync", "od")
        p.emit(es.enter_context(nc.Block()))
    return nc, cx.TAP


def stream_weight(cx, wname, buf_name, cols=None, queue="sync"):
    p = cx.p
    wt, wv = cx.ws_rd[buf_name]
    p.wait(queue, "ts", wt)
    p.wait(queue, "vs", wv)
    p.wait(queue, "ss")
    buf = cx.gv(buf_name)
    src = cx.WB[wname]
    if cols is not None:
        buf = buf[:, :, cols[0]:cols[1]]
    src = src.rearrange("(k p) m -> p k m", p=128)
    sem = {"sync": "ds", "scalar": "dc", "gpsimd": "dg"}[queue]
    v = p.dma(queue, buf, src, sem)
    return v, sem


def mark_ws_used(cx, buf_name):
    cx.ws_rd[buf_name] = (cx.p.val("ts"), cx.p.val("vs"))


def phase1(cx, b):
    p, gv, pb = cx.p, cx.gv, cx.pb
    x0tm = [gv("X0TM0"), gv("X0TM1")]
    rd = [(0, 0, 0), (0, 0, 0)]
    hb_rd_ts = 0
    for tt in range(16):
        bi = tt % 2
        t_, v_, s_ = rd[bi]
        p.wait("sync", "ts", t_)
        p.wait("sync", "vs", v_)
        p.wait("sync", "ss", s_)
        ld = p.dma("sync", x0tm[bi], cx.x_in[b, tt * 128:(tt + 1) * 128, :], "ds")
        p.wait("vector", "ds", ld)
        p.wait("vector", "ss")
        p.op("vector", lambda e, bi=bi: e.tensor_tensor(out=gv("CONVO")[:, 0:D], in0=x0tm[bi], in1=x0tm[bi], op=OP.mult), inc="vs")
        p.op("vector", lambda e, bi=bi: e.reduce_sum(out=gv("STATS")[:, 0:1], in_=x0tm[bi], axis=AX.X), inc="vs")
        p.op("vector", lambda e: e.reduce_sum(out=gv("STATS")[:, 1:2], in_=gv("CONVO")[:, 0:D], axis=AX.X), inc="vs")
        p.selfw("vector")
        p.op("vector", lambda e: e.tensor_scalar(out=gv("STATS")[:, 2:3], in0=gv("STATS")[:, 0:1], scalar1=1.0 / D, scalar2=None, op0=OP.mult), inc="vs")
        p.op("vector", lambda e: e.tensor_scalar(out=gv("STATS")[:, 3:4], in0=gv("STATS")[:, 1:2], scalar1=1.0 / D, scalar2=None, op0=OP.mult), inc="vs")
        p.selfw("vector")
        p.op("vector", lambda e: e.tensor_tensor(out=gv("STATS")[:, 4:5], in0=gv("STATS")[:, 2:3], in1=gv("STATS")[:, 2:3], op=OP.mult), inc="vs")
        p.selfw("vector")
        p.op("vector", lambda e: e.tensor_sub(out=gv("STATS")[:, 5:6], in0=gv("STATS")[:, 3:4], in1=gv("STATS")[:, 4:5]), inc="vs")
        p.wait("scalar", "vs")
        p.op("scalar", lambda e: e.activation(out=gv("STATS")[:, 6:7], in_=gv("STATS")[:, 5:6], func=AF.Ln, bias=EPS_LN), inc="ss")
        p.selfw("scalar")
        p.op("scalar", lambda e: e.activation(out=gv("STATS")[:, 7:8], in_=gv("STATS")[:, 6:7], func=AF.Exp, scale=-0.5), inc="ss")
        p.wait("vector", "ss")
        p.op("vector", lambda e, bi=bi: e.tensor_scalar(
            out=gv("HTM"), in0=x0tm[bi], scalar1=gv("STATS")[:, 2:3],
            scalar2=gv("STATS")[:, 7:8], op0=OP.subtract, op1=OP.mult), inc="vs")
        p.wait("vector", "ts", hb_rd_ts)
        hbv = p.op("vector", lambda e: e.tensor_copy(out=gv("HB"), in_=gv("HTM")), inc="vs")
        for ft in range(FTL):
            bank = 4 + (ft % 2)
            psv = pb[bank][:, 0:64].bitcast(BF)
            cx.mm_into(bank, [lambda e, ft=ft, psv=psv: e.transpose(
                out=psv, in_=gv("HB")[:, ft * 128:(ft + 1) * 128], identity=gv("IDBF"))],
                extra_waits=[("vs", hbv), ("dg", cx.small_casts), ("ds", cx.preload)])
            cx.bank_read("vector", bank, lambda e, ft=ft, psv=psv, tt=tt: e.tensor_copy(
                out=gv("HREG")[:, ft, tt * 128:(tt + 1) * 128], in_=psv), "vs")
        hb_rd_ts = p.val("ts")
        for ft in range(FTL):
            bank = 6 + (ft % 2)
            psv = pb[bank][:, 0:128]
            cx.mm_into(bank, [lambda e, ft=ft, psv=psv, bi=bi: e.transpose(
                out=psv, in_=x0tm[bi][:, ft * 128:(ft + 1) * 128], identity=gv("IDF"))])
            cx.bank_read("vector", bank, lambda e, ft=ft, psv=psv: e.tensor_copy(
                out=gv("X0S")[:, ft, :], in_=psv), "vs")
        rd[bi] = (p.val("ts"), p.val("vs"), p.val("ss"))
        xsv = p.val("vs")
        p.wait("scalar", "vs", xsv)
        p.dma("scalar", cx.SC["X0T"][b, :, tt * 128:(tt + 1) * 128]
              .rearrange("(f p) t -> p f t", p=128), gv("X0S"), "dc")
        p.wait("vector", "dc")
    cx.p1_done = p.val("vs")


def phase2(cx, b):
    p, gv, pb = cx.p, cx.gv, cx.pb
    p.wait("sync", "dg", cx.small_casts)

    for wname, cw, cb, is_v, dst, wbuf in [
            ("aWq", "aqcw", "aqcb", False, "QT", "WS0"),
            ("aWk", "akcw", "akcb", False, "KT", "WS1"),
            ("aWv", "avcw", "avcb", True, "VT", "WS0")]:
        ldv, ldsem = stream_weight(cx, wname, wbuf)
        ws = gv(wbuf)
        for ft in range(FTL):
            for nt in range(NT):
                bank = nt % 2
                psv = pb[bank][:, 0:512]
                fns = [lambda e, kt=kt, nt=nt, ft=ft, psv=psv, ws=ws: e.matmul(
                    out=psv, lhsT=ws[:, kt, ft * 128:(ft + 1) * 128],
                    rhs=gv("HREG")[:, kt, nt * 512:(nt + 1) * 512],
                    start=(kt == 0), stop=(kt == FTL - 1)) for kt in range(FTL)]
                cx.mm_into(bank, fns, extra_waits=[(ldsem, ldv), ("vs", cx.p1_done)])
                cx.bank_read("vector", bank, lambda e, nt=nt, psv=psv: e.tensor_copy(
                    out=gv("CONVP")[:, 3 + nt * 512: 3 + (nt + 1) * 512], in_=psv), "vs")
            p.op("vector", lambda e: e.memset(gv("CONVP")[:, 0:3], 0.0), inc="vs")
            p.op("vector", lambda e, cw=cw, ft=ft: e.tensor_scalar(
                out=gv("CONVO"), in0=gv("CONVP")[:, 0:T], scalar1=gv(cw)[:, ft, 0:1],
                scalar2=None, op0=OP.mult), inc="vs")
            p.selfw("vector")
            p.op("vector", lambda e, cw=cw, ft=ft: e.scalar_tensor_tensor(
                out=gv("CONVQ"), in0=gv("CONVP")[:, 1:T + 1], scalar=gv(cw)[:, ft, 1:2],
                in1=gv("CONVO"), op0=OP.mult, op1=OP.add), inc="vs")
            p.selfw("vector")
            p.op("vector", lambda e, cw=cw, ft=ft: e.scalar_tensor_tensor(
                out=gv("CONVO"), in0=gv("CONVP")[:, 2:T + 2], scalar=gv(cw)[:, ft, 2:3],
                in1=gv("CONVQ"), op0=OP.mult, op1=OP.add), inc="vs")
            p.selfw("vector")
            cvv = p.op("vector", lambda e, cw=cw, ft=ft: e.scalar_tensor_tensor(
                out=gv("CONVQ"), in0=gv("CONVP")[:, 3:T + 3], scalar=gv(cw)[:, ft, 3:4],
                in1=gv("CONVO"), op0=OP.mult, op1=OP.add), inc="vs")
            p.waitf("scalar", "vs", cvv)
            sv_ = p.op("scalar", lambda e, cb=cb, ft=ft: e.activation(
                out=gv("CONVO"), in_=gv("CONVQ"), func=AF.Silu,
                bias=gv(cb)[:, ft:ft + 1]), inc="ss")
            if is_v:
                p.wait("vector", "ss", sv_)
                p.wait("vector", "dc")
                vv = p.op("vector", lambda e: e.tensor_copy(out=gv("RB"), in_=gv("CONVO")), inc="vs")
                p.wait("scalar", "vs", vv)
                p.dma("scalar", cx.SC[dst][b, ft * 128:(ft + 1) * 128, :], gv("RB"), "dc")
            else:
                p.wait("vector", "ss", sv_)
                rbv = p.op("vector", lambda e: e.tensor_copy(out=gv("RB"), in_=gv("CONVO")), inc="vs")
                p.wait("vector", "dc")
                for nt in range(NT):
                    ba, bb_, bs, bc = 2, 3, 6, 7
                    cx.mm_into(ba, [lambda e, nt=nt: e.matmul(
                        out=pb[ba][:, 0:512], lhsT=gv("PERMA"),
                        rhs=gv("RB")[:, nt * 512:(nt + 1) * 512], start=True, stop=True)],
                        extra_waits=[("vs", rbv)])
                    cx.mm_into(bb_, [lambda e, nt=nt: e.matmul(
                        out=pb[bb_][:, 0:512], lhsT=gv("PERMB"),
                        rhs=gv("RB")[:, nt * 512:(nt + 1) * 512], start=True, stop=True)])
                    cx.bank_read("vector", ba, lambda e, nt=nt: e.tensor_tensor(
                        out=gv("ROPE1"), in0=pb[ba][:, 0:512],
                        in1=gv("CROPE")[:, nt * 512:(nt + 1) * 512], op=OP.mult), "vs")
                    cx.bank_read("vector", bb_, lambda e, nt=nt: e.tensor_tensor(
                        out=gv("ROPE2"), in0=pb[bb_][:, 0:512],
                        in1=gv("SROPE")[:, nt * 512:(nt + 1) * 512], op=OP.mult), "vs")
                    p.selfw("vector")
                    rov = p.op("vector", lambda e: e.tensor_add(
                        out=gv("ROPEO"), in0=gv("ROPE1"), in1=gv("ROPE2")), inc="vs")
                    p.waitf("scalar", "vs", rov)
                    sqv = p.op("scalar", lambda e: e.activation(
                        out=gv("SQB"), in_=gv("ROPEO"), func=AF.Square), inc="ss")
                    cx.mm_into(bs, [lambda e: e.matmul(
                        out=pb[bs][0:2, 0:512], lhsT=gv("BD64"), rhs=gv("SQB"),
                        start=True, stop=True)], extra_waits=[("ss", sqv)])
                    if wname == "aWq" and ft == 0 and nt == 0 and "rope0" in cx.taps:
                        cx.bank_read("vector", bs, lambda e: e.tensor_copy(
                            out=gv("CONVP")[0:2, 512:1024], in_=pb[bs][0:2, 0:512]), "vs")
                        dv0 = p.val("vs")
                        p.wait("sync", "vs", dv0)
                        p.dma("sync", cx.TAP["psum0"], gv("CONVP")[0:2, 512:1024], "od")
                        p.wait("vector", "od")
                    cx.bank_read("scalar", bs, lambda e: e.activation(
                        out=gv("RSTD2"), in_=pb[bs][0:2, 0:512], func=AF.Ln, bias=EPS_QK), "ss")
                    p.selfw("scalar")
                    rsv = p.op("scalar", lambda e: e.activation(out=gv("RSTD2B"), in_=gv("RSTD2"), func=AF.Exp, scale=-0.5), inc="ss")
                    cx.mm_into(bc, [lambda e: e.matmul(
                        out=pb[bc][:, 0:512], lhsT=gv("BD64T"), rhs=gv("RSTD2B"),
                        start=True, stop=True)], extra_waits=[("ss", rsv)])
                    cx.bank_read("vector", bc, lambda e, nt=nt: e.tensor_tensor(
                        out=gv("STG")[:, nt, :], in0=gv("ROPEO"), in1=pb[bc][:, 0:512],
                        op=OP.mult), "vs")
                    if wname == "aWq" and ft == 0 and nt == 0 and "rope0" in cx.taps:
                        cx.bank_read("vector", bc, lambda e: e.tensor_copy(
                            out=gv("CONVP")[:, 0:512], in_=pb[bc][:, 0:512]), "vs")
                        dv = p.val("vs")
                        p.wait("sync", "vs", dv)
                        p.dma("sync", cx.TAP["rope0"], gv("ROPEO"), "od")
                        p.dma("sync", cx.TAP["bcast0"], gv("CONVP")[:, 0:512], "od")
                        p.dma("sync", cx.TAP["rstd0"], gv("RSTD2"), "od")
                        p.dma("sync", cx.TAP["sqb0"], gv("SQB"), "od")
                        p.wait("vector", "od")
                        p.wait("scalar", "od")
                        p.wait("tensor", "od")
                stv = p.val("vs")
                p.wait("scalar", "vs", stv)
                p.dma("scalar", cx.SC[dst][b, ft * 128:(ft + 1) * 128, :],
                      gv("STG").rearrange("p a b -> p (a b)"), "dc")
        mark_ws_used(cx, wbuf)

    ldv, ldsem = stream_weight(cx, "aWg", "WS1")
    for ft in range(FTL):
        p.wait("vector", "dc")
        for nt in range(NT):
            bank = nt % 2
            psv = pb[bank][:, 0:512]
            fns = [lambda e, kt=kt, nt=nt, ft=ft, psv=psv: e.matmul(
                out=psv, lhsT=gv("WS1")[:, kt, ft * 128:(ft + 1) * 128],
                rhs=gv("HREG")[:, kt, nt * 512:(nt + 1) * 512],
                start=(kt == 0), stop=(kt == FTL - 1)) for kt in range(FTL)]
            cx.mm_into(bank, fns, extra_waits=[(ldsem, ldv)])
            cx.bank_read("vector", bank, lambda e, nt=nt, psv=psv: e.tensor_copy(
                out=gv("GSTG")[:, nt * 512:(nt + 1) * 512], in_=psv), "vs")
        gsv = p.val("vs")
        p.wait("scalar", "vs", gsv)
        p.dma("scalar", cx.SC["GT"][b, ft * 128:(ft + 1) * 128, :], gv("GSTG"), "dc")
    mark_ws_used(cx, "WS1")

    ldv, ldsem = stream_weight(cx, "abw", "WS0", cols=(0, 16))
    ldv2, _ = stream_weight(cx, "agkw", "WS0", cols=(16, 32))
    for nt in range(NT):
        bank = nt % 2
        psv = pb[bank][0:16, 0:512]
        fns = [lambda e, kt=kt, nt=nt, psv=psv: e.matmul(
            out=psv, lhsT=gv("WS0")[:, kt, 0:16],
            rhs=gv("HREG")[:, kt, nt * 512:(nt + 1) * 512],
            start=(kt == 0), stop=(kt == FTL - 1)) for kt in range(FTL)]
        cx.mm_into(bank, fns, extra_waits=[(ldsem, ldv2)])
        cx.bank_read("scalar", bank, lambda e, nt=nt, psv=psv: e.activation(
            out=gv("RWbeta")[:, nt * 512:(nt + 1) * 512], in_=psv, func=AF.Sigmoid,
            bias=gv("abb")), "ss")
        bank2 = 2 + nt % 2
        psv2 = pb[bank2][0:16, 0:512]
        fns = [lambda e, kt=kt, nt=nt, psv2=psv2: e.matmul(
            out=psv2, lhsT=gv("WS0")[:, kt, 16:32],
            rhs=gv("HREG")[:, kt, nt * 512:(nt + 1) * 512],
            start=(kt == 0), stop=(kt == FTL - 1)) for kt in range(FTL)]
        cx.mm_into(bank2, fns)
        cx.bank_read("vector", bank2, lambda e, nt=nt, psv2=psv2: e.tensor_scalar(
            out=gv("RWz")[:, nt * 512:(nt + 1) * 512], in0=psv2, scalar1=gv("gkbias"),
            scalar2=None, op0=OP.add), "vs")
    mark_ws_used(cx, "WS0")
    zv = p.val("vs")
    p.wait("scalar", "vs", zv)
    p.op("scalar", lambda e: e.activation(out=gv("RWt"), in_=gv("RWz"), func=AF.Abs), inc="ss")
    p.selfw("scalar")
    p.op("scalar", lambda e: e.activation(out=gv("RWt"), in_=gv("RWt"), func=AF.Exp, scale=-1.0), inc="ss")
    p.selfw("scalar")
    p.op("scalar", lambda e: e.activation(out=gv("RWt"), in_=gv("RWt"), func=AF.Ln, bias=1.0), inc="ss")
    p.op("scalar", lambda e: e.activation(out=gv("RWz"), in_=gv("RWz"), func=AF.Relu), inc="ss")
    p.wait("vector", "ss")
    p.op("vector", lambda e: e.tensor_add(out=gv("RWz"), in0=gv("RWz"), in1=gv("RWt")), inc="vs")
    p.selfw("vector")
    p.op("vector", lambda e: e.tensor_scalar(out=gv("RWglog"), in0=gv("RWz"), scalar1=gv("negeA"), scalar2=None, op0=OP.mult), inc="vs")
    p.selfw("vector")
    p.op("vector", lambda e: e.tensor_tensor_scan(out=gv("Pg"), data0=gv("RST"), data1=gv("RWglog"), initial=0.0, op0=OP.mult, op1=OP.add), inc="vs")
    p.selfw("vector")
    p.op("vector", lambda e: e.tensor_sub(out=gv("Pgp"), in0=gv("Pg"), in1=gv("RWglog")), inc="vs")
    spv = p.val("vs")
    p.wait("scalar", "vs", spv)
    p.op("scalar", lambda e: e.activation(out=gv("RWlnb"), in_=gv("RWbeta"), func=AF.Ln), inc="ss")
    p.op("scalar", lambda e: e.activation(out=gv("Pegp"), in_=gv("Pgp"), func=AF.Exp), inc="ss")
    p.wait("vector", "ss")
    p.op("vector", lambda e: e.tensor_add(out=gv("Pgb"), in0=gv("Pg"), in1=gv("RWlnb")), inc="vs")
    for c in range(NCH):
        p.op("vector", lambda e, c=c: e.tensor_scalar(
            out=gv("RWc")[:, c * C:(c + 1) * C], in0=gv("Pg")[:, c * C:(c + 1) * C],
            scalar1=0.0, scalar2=gv("Pg")[:, c * C + C - 1:c * C + C],
            op0=OP.mult, op1=OP.add), inc="vs")
    rowsv = p.val("vs")
    for c in range(NCH):
        for srcn, dstn in [("Pg", "gT"), ("Pgb", "gbT"), ("RWc", "gCT"), ("RWbeta", "bT")]:
            bank = c % 2
            psv = pb[bank][:, 0:16]
            cx.mm_into(bank, [lambda e, c=c, srcn=srcn, psv=psv: e.transpose(
                out=psv, in_=gv(srcn)[:, c * C:(c + 1) * C],
                identity=gv("IDF")[0:16, 0:16])],
                extra_waits=[("vs", rowsv)])
            cx.bank_read("vector", bank, lambda e, c=c, dstn=dstn, psv=psv: e.tensor_copy(
                out=gv(dstn)[:, c, :], in_=psv), "vs")
    p.op("vector", lambda e: e.tensor_sub(
        out=gv("lamT").rearrange("p a b -> p (a b)"),
        in0=gv("gCT").rearrange("p a b -> p (a b)"),
        in1=gv("gT").rearrange("p a b -> p (a b)")), inc="vs")
    lv = p.val("vs")
    p.waitf("scalar", "vs", lv)
    p.op("scalar", lambda e: e.activation(out=gv("lamT").rearrange("p a b -> p (a b)"),
                                          in_=gv("lamT").rearrange("p a b -> p (a b)"), func=AF.Exp), inc="ss")
    p.op("scalar", lambda e: e.activation(out=gv("bPT").rearrange("p a b -> p (a b)"),
                                          in_=gv("gbT").rearrange("p a b -> p (a b)"), func=AF.Exp), inc="ss")
    p.op("scalar", lambda e: e.activation(out=gv("LAMH").rearrange("p a b -> p (a b)"),
                                          in_=gv("gCT").rearrange("p a b -> p (a b)"), func=AF.Exp), inc="ss")
    cx.rows_ss = p.val("ss")


def phase3(cx, b):
    p, gv, pb = cx.p, cx.gv, cx.pb
    p.wait("vector", "ss", cx.rows_ss)
    p.wait("sync", "dc")
    for h in range(H):
        p.wait("sync", "ts")
        p.wait("sync", "vs")
        p.wait("sync", "ss")
        lq = p.dma("sync", gv("QTL"), cx.SC["QT"][b, h * 64:(h + 1) * 64, :], "ds")
        lk = p.dma("sync", gv("KTL"), cx.SC["KT"][b, h * 64:(h + 1) * 64, :], "ds")
        lv_ = p.dma("sync", gv("VTL"), cx.SC["VT"][b, h * 64:(h + 1) * 64, :], "ds")
        lg = p.dma("sync", gv("GTld"), cx.SC["GT"][b, h * 64:(h + 1) * 64, :], "ds")
        p.dma("sync", gv("ROW0G"), gv("Pg")[h:h + 1, :], "ds")
        p.dma("sync", gv("ROW0GB"), gv("Pgb")[h:h + 1, :], "ds")
        p.dma("sync", gv("ROW0GP"), gv("Pgp")[h:h + 1, :], "ds")
        reg = p.dma("sync", gv("ROW0EG"), gv("Pegp")[h:h + 1, :], "ds")
        rowsrd = p.val("ds")
        for c in range(NCH):
            bank = c % 2
            psv = pb[bank][:, 0:32].bitcast(BF)
            cx.mm_into(bank, [lambda e, c=c, psv=psv: e.transpose(
                out=psv, in_=gv("KTL")[:, c * C:(c + 1) * C], identity=gv("IDBF")[0:64, 0:64])],
                extra_waits=[("ds", lk)])
            cx.bank_read("vector", bank, lambda e, c=c, psv=psv: e.tensor_copy(
                out=gv("KTOK")[:, c, :], in_=psv), "vs")
            psv2 = pb[bank][:, 32:64].bitcast(BF)
            cx.mm_into(bank, [lambda e, c=c, psv2=psv2: e.transpose(
                out=psv2, in_=gv("VTL")[:, c * C:(c + 1) * C], identity=gv("IDBF")[0:64, 0:64])],
                extra_waits=[("ds", lv_)])
            cx.bank_read("vector", bank, lambda e, c=c, psv2=psv2: e.tensor_copy(
                out=gv("VTOK")[:, c, :], in_=psv2), "vs")
        for c in range(NCH):
            _chunk(cx, b, h, c, rowsrd)
        _scan_and_gate(cx, b, h)


def _chunk(cx, b, h, c, rowsrd):
    p, gv, pb = cx.p, cx.gv, cx.pb
    cs = slice(c * C, (c + 1) * C)
    gcol = gv("gT")[:, c, h:h + 1]
    gbcol = gv("gbT")[:, c, h:h + 1]
    bcol = gv("bT")[:, c, h:h + 1]
    lamcol = gv("lamT")[:, c, h:h + 1]
    bpcol = gv("bPT")[:, c, h:h + 1]
    BK, BQ, BB, BP, BPT, BA, BM, BD = 0, 1, 2, 3, 4, 5, 6, 7
    cx.mm_into(BK, [lambda e: e.matmul(out=pb[BK][:, 0:128], lhsT=gv("KTL")[:, cs],
                                       rhs=gv("KTL")[:, cs], start=True, stop=True)])
    cx.mm_into(BQ, [lambda e: e.matmul(out=pb[BQ][:, 0:128], lhsT=gv("KTL")[:, cs],
                                       rhs=gv("QTL")[:, cs], start=True, stop=True)])
    cx.mm_into(BB, [lambda e: e.matmul(out=pb[BB][:, 0:128], lhsT=gv("ONESF")[0:1, :],
                                       rhs=gv("ROW0G")[0:1, cs], start=True, stop=True)],
               extra_waits=[("ds", rowsrd)])
    cx.bank_read("vector", BB, lambda e: e.scalar_tensor_tensor(
        out=gv("ARG"), in0=pb[BB][:, 0:128], scalar=gbcol, in1=gv("MASKC"),
        op0=OP.subtract, op1=OP.add), "vs")
    p.waitf("scalar", "vs")
    ebv = p.op("scalar", lambda e: e.activation(out=gv("EbL"), in_=gv("ARG"), func=AF.Exp, scale=-1.0), inc="ss")
    p.wait("vector", "ss", ebv)
    cx.bank_read("vector", BK, lambda e: e.scalar_tensor_tensor(
        out=gv("Nt"), in0=pb[BK][:, 0:128], scalar=-1.0, in1=gv("EbL"),
        op0=OP.mult, op1=OP.mult), "vs")
    cx.mm_into(BB, [lambda e: e.matmul(out=pb[BB][:, 128:256], lhsT=gv("ONESF")[0:1, :],
                                       rhs=gv("ROW0GB")[0:1, cs], start=True, stop=True)])
    cx.bank_read("vector", BB, lambda e: e.scalar_tensor_tensor(
        out=gv("ARG"), in0=pb[BB][:, 128:256], scalar=gcol, in1=gv("MASKD"),
        op0=OP.subtract, op1=OP.add), "vs")
    p.waitf("scalar", "vs")
    env = p.op("scalar", lambda e: e.activation(out=gv("EbL"), in_=gv("ARG"), func=AF.Exp), inc="ss")
    p.wait("vector", "ss", env)
    cx.bank_read("vector", BK, lambda e: e.scalar_tensor_tensor(
        out=gv("NTt"), in0=pb[BK][:, 0:128], scalar=-1.0, in1=gv("EbL"),
        op0=OP.mult, op1=OP.mult), "vs")
    cx.bank_read("vector", BB, lambda e: e.scalar_tensor_tensor(
        out=gv("ARG"), in0=pb[BB][:, 128:256], scalar=gcol, in1=gv("MASKB"),
        op0=OP.subtract, op1=OP.add), "vs")
    p.waitf("scalar", "vs")
    env2 = p.op("scalar", lambda e: e.activation(out=gv("EbL"), in_=gv("ARG"), func=AF.Exp), inc="ss")
    p.wait("vector", "ss", env2)
    cx.bank_read("vector", BK, lambda e: e.scalar_tensor_tensor(
        out=gv("NToff"), in0=pb[BK][:, 0:128], scalar=-1.0, in1=gv("EbL"),
        op0=OP.mult, op1=OP.mult), "vs")
    cx.mm_into(BB, [lambda e: e.matmul(out=pb[BB][:, 256:384], lhsT=gv("ONESF")[0:1, :],
                                       rhs=gv("ROW0GP")[0:1, cs], start=True, stop=True)])
    cx.bank_read("vector", BB, lambda e: e.scalar_tensor_tensor(
        out=gv("ARG"), in0=pb[BB][:, 256:384], scalar=gcol, in1=gv("MASKB"),
        op0=OP.subtract, op1=OP.add), "vs")
    p.waitf("scalar", "vs")
    emv = p.op("scalar", lambda e: e.activation(out=gv("EbL"), in_=gv("ARG"), func=AF.Exp), inc="ss")
    p.wait("vector", "ss", emv)
    cx.bank_read("vector", BQ, lambda e: e.tensor_tensor(
        out=gv("MTc"), in0=pb[BQ][:, 0:128], in1=gv("EbL"), op=OP.mult), "vs")
    cx.mm_into(BB, [lambda e: e.matmul(out=pb[BB][0:64, 384:512], lhsT=gv("ONESF")[0:1, 0:64],
                                       rhs=gv("ROW0EG")[0:1, cs], start=True, stop=True)])
    cx.bank_read("vector", BB, lambda e: e.tensor_tensor(
        out=gv("QSc")[0:64, :], in0=gv("QTL")[:, cs], in1=pb[BB][0:64, 384:512], op=OP.mult), "vs")
    p.op("vector", lambda e: e.tensor_scalar(
        out=gv("Rc")[:, 0:64], in0=gv("VTOK")[:, c, :], scalar1=bcol, scalar2=None, op0=OP.mult), inc="vs")
    p.op("vector", lambda e: e.tensor_scalar(
        out=gv("Rc")[:, 64:128], in0=gv("KTOK")[:, c, :], scalar1=bpcol, scalar2=None, op0=OP.mult), inc="vs")
    X, Xo = gv("Xc"), gv("Xd")
    Pcur, PcurT = gv("Nt"), gv("NTt")
    pn = [(gv("Pa"), gv("PaT")), (gv("Pb"), gv("PbT"))]
    powTs = [PcurT]
    p.op("vector", lambda e: e.tensor_copy(out=X, in_=gv("Rc")), inc="vs")
    for k in range(6):
        sv = p.val("vs")
        cx.mm_into(BA, [lambda e, PcurT=PcurT, X=X: e.matmul(
            out=pb[BA][:, 0:128], lhsT=PcurT, rhs=X, start=True, stop=True)],
            extra_waits=[("vs", sv)])
        cx.bank_read("vector", BA, lambda e, X=X, Xo=Xo: e.tensor_tensor(
            out=Xo, in0=X, in1=pb[BA][:, 0:128], op=OP.add), "vs")
        X, Xo = Xo, X
        if k < 5:
            cx.mm_into(BP, [lambda e, PcurT=PcurT, Pcur=Pcur: e.matmul(
                out=pb[BP][:, 0:128], lhsT=PcurT, rhs=Pcur, start=True, stop=True)])
            cx.mm_into(BPT, [lambda e, PcurT=PcurT, Pcur=Pcur: e.matmul(
                out=pb[BPT][:, 0:128], lhsT=Pcur, rhs=PcurT, start=True, stop=True)])
            Pnx, PnxT = pn[k % 2]
            cx.bank_read("vector", BP, lambda e, Pnx=Pnx: e.tensor_copy(out=Pnx, in_=pb[BP][:, 0:128]), "vs")
            cx.bank_read("vector", BPT, lambda e, PnxT=PnxT: e.tensor_copy(out=PnxT, in_=pb[BPT][:, 0:128]), "vs")
            Pcur, PcurT = Pnx, PnxT
            powTs.append(PcurT)
    Y = X
    zbufs = [Xo, gv("Rc")]
    yv = p.val("vs")
    cx.mm_into(BA, [lambda e, Y=Y: e.matmul(
        out=pb[BA][64:128, 128:256], lhsT=gv("NToff")[0:64, 64:128], rhs=Y[0:64, :],
        start=True, stop=True, tile_position=(0, 64))], extra_waits=[("vs", yv)])
    zc = zbufs[0]
    cx.bank_read("vector", BA, lambda e, zc=zc: e.tensor_copy(
        out=zc[64:128, :], in_=pb[BA][64:128, 128:256]), "vs")
    for k in range(6):
        zn = zbufs[1 - k % 2]
        zv = p.val("vs")
        cx.mm_into(BA, [lambda e, k=k, zc=zc: e.matmul(
            out=pb[BA][64:128, 128:256], lhsT=powTs[k][64:128, 64:128], rhs=zc[64:128, :],
            start=True, stop=True, tile_position=(64, 64))], extra_waits=[("vs", zv)])
        cx.bank_read("vector", BA, lambda e, zc=zc, zn=zn: e.tensor_tensor(
            out=zn[64:128, :], in0=zc[64:128, :], in1=pb[BA][64:128, 128:256], op=OP.add), "vs")
        zc = zn
    p.op("vector", lambda e, Y=Y, zc=zc: e.tensor_tensor(
        out=Y[64:128, :], in0=Y[64:128, :], in1=zc[64:128, :], op=OP.add), inc="vs")
    X = Y
    p.op("vector", lambda e: e.tensor_scalar(
        out=gv("KPc")[:, 0:64], in0=gv("KTOK")[:, c, :], scalar1=lamcol, scalar2=None, op0=OP.mult), inc="vs")
    kpv = p.val("vs")
    cx.mm_into(BM, [lambda e, X=X: e.matmul(out=pb[BM][0:64, 0:64], lhsT=gv("KPc")[:, 0:64],
                                            rhs=X[:, 0:64], start=True, stop=True)],
               extra_waits=[("vs", kpv)])
    cx.mm_into(BB, [lambda e, X=X: e.matmul(out=pb[BB][0:64, 0:64], lhsT=X[:, 64:128],
                                            rhs=gv("KPc")[:, 0:64], start=True, stop=True)])
    cx.mm_into(BP, [lambda e, X=X: e.matmul(out=pb[BP][0:64, 0:128], lhsT=X[:, 64:128],
                                            rhs=gv("MTc"), start=True, stop=True)])
    cx.mm_into(BPT, [lambda e, X=X: e.matmul(out=pb[BPT][0:64, 0:128], lhsT=X[:, 0:64],
                                             rhs=gv("MTc"), start=True, stop=True)])
    p.op("vector", lambda e: e.tensor_tensor(out=gv("QKbf")[0:64, :], in0=gv("QTL")[:, cs],
                                             in1=gv("KTL")[:, cs], op=OP.mult), inc="vs")
    qkv = p.val("vs")
    cx.mm_into(BD, [lambda e: e.matmul(out=pb[BD][0:1, 0:128], lhsT=gv("ONESB")[0:64, 0:1],
                                       rhs=gv("QKbf")[0:64, :], start=True, stop=True)],
               extra_waits=[("vs", qkv)])
    cx.bank_read("vector", BD, lambda e: e.tensor_scalar(
        out=gv("QKbf")[0:1, :], in0=pb[BD][0:1, 0:128], scalar1=gv("aD")[0:1, h:h + 1],
        scalar2=None, op0=OP.mult), "vs")
    qdv = p.val("vs")
    cx.mm_into(BD, [lambda e: e.matmul(out=pb[BD][0:64, 128:256], lhsT=gv("ONESB")[0:1, 0:64],
                                       rhs=gv("QKbf")[0:1, :], start=True, stop=True)],
               extra_waits=[("vs", qdv)])
    cx.bank_read("vector", BD, lambda e: e.tensor_tensor(
        out=gv("BQK")[0:64, :], in0=pb[BD][0:64, 128:256], in1=gv("VTL")[:, cs], op=OP.mult), "vs")
    cx.bank_read("vector", BPT, lambda e, c=c: e.tensor_tensor(
        out=gv("MAT1")[:, c * 128:(c + 1) * 128], in0=pb[BPT][0:64, 0:128], in1=gv("BQK")[0:64, :],
        op=OP.add), "vs")
    cx.bank_read("vector", BP, lambda e, c=c: e.tensor_tensor(
        out=gv("MAT2T")[:, c * 128:(c + 1) * 128], in0=gv("QSc")[0:64, :], in1=pb[BP][0:64, 0:128],
        op=OP.subtract), "vs")
    p.op("vector", lambda e, c=c, h=h: e.tensor_scalar(
        out=gv("LAMI")[0:64, 0:64], in0=gv("IDF")[0:64, 0:64], scalar1=gv("LAMH")[0:64, c, h:h + 1],
        scalar2=None, op0=OP.mult), inc="vs")
    cx.bank_read("vector", BB, lambda e, c=c: e.scalar_tensor_tensor(
        out=gv("ATm")[:, c * 64:(c + 1) * 64], in0=pb[BB][0:64, 0:64], scalar=-1.0,
        in1=gv("LAMI")[0:64, 0:64], op0=OP.mult, op1=OP.add), "vs")
    cx.bank_read("vector", BM, lambda e, c=c: e.tensor_copy(
        out=gv("BCm")[:, c * 64:(c + 1) * 64], in_=pb[BM][0:64, 0:64]), "vs")
    if b == 0 and h == 0 and c == 0 and "chk" in cx.taps:
        dv = p.val("vs")
        p.wait("sync", "vs", dv)
        p.wait("sync", "ss")
        for nm, ap in [("cNt", gv("Nt")), ("cMTc", gv("MTc")), ("cX", X),
                       ("cRc", gv("Rc")), ("cQSc", gv("QSc")), ("cBQK", gv("BQK")),
                       ("cMAT1", gv("MAT1")[:, 0:128]), ("cMAT2T", gv("MAT2T")[:, 0:128]),
                       ("cATm", gv("ATm")[:, 0:64]), ("cBCm", gv("BCm")[:, 0:64]),
                       ("cKTOK", gv("KTOK").rearrange("p a b -> p (a b)")),
                       ("cVTOK", gv("VTOK").rearrange("p a b -> p (a b)")),
                       ("cgT", gv("gT").rearrange("p a b -> p (a b)")),
                       ("cgbT", gv("gbT").rearrange("p a b -> p (a b)")),
                       ("cbT", gv("bT").rearrange("p a b -> p (a b)")),
                       ("clamT", gv("lamT").rearrange("p a b -> p (a b)")),
                       ("cbPT", gv("bPT").rearrange("p a b -> p (a b)")),
                       ("cLAMH", gv("LAMH").rearrange("p a b -> p (a b)")),
                       ("cROW0G", gv("ROW0G")), ("cROW0EG", gv("ROW0EG"))]:
            p.dma("sync", cx.TAP[nm], ap, "od")
        p.wait("vector", "od")
        p.wait("scalar", "od")
        p.wait("tensor", "od")


def _scan_and_gate(cx, b, h):
    p, gv, pb = cx.p, cx.gv, cx.pb
    sst = [gv("SST0"), gv("SST1")]
    p.op("vector", lambda e: e.memset(sst[0], 0.0), inc="vs")
    for c in range(NCH):
        cur, nxt = sst[c % 2], sst[1 - c % 2]
        sv = p.val("vs")
        cx.mm_into(0, [lambda e, cur=cur, c=c: e.matmul(
            out=pb[0][0:64, 0:128], lhsT=cur, rhs=gv("MAT2T")[:, c * 128:(c + 1) * 128],
            start=True, stop=True)], extra_waits=[("vs", sv)])
        cx.bank_read("vector", 0, lambda e, c=c: e.tensor_tensor(
            out=gv("OH")[:, c * 128:(c + 1) * 128], in0=pb[0][0:64, 0:128],
            in1=gv("MAT1")[:, c * 128:(c + 1) * 128], op=OP.add), "vs")
        if c < NCH - 1:
            cx.mm_into(1, [lambda e, cur=cur, c=c: e.matmul(
                out=pb[1][0:64, 0:64], lhsT=gv("ATm")[:, c * 64:(c + 1) * 64],
                rhs=cur, start=True, stop=True)], extra_waits=[("vs", sv)])
            cx.bank_read("vector", 1, lambda e, c=c, nxt=nxt: e.tensor_tensor(
                out=nxt, in0=pb[1][0:64, 0:64], in1=gv("BCm")[:, c * 64:(c + 1) * 64],
                op=OP.add), "vs")
    if b == 0 and h == 0 and "chk" in cx.taps:
        dv = p.val("vs")
        p.wait("sync", "vs", dv)
        p.dma("sync", cx.TAP["cOH"], gv("OH"), "od")
        p.wait("vector", "od")
    for nt in range(NT):
        sl = slice(nt * 512, (nt + 1) * 512)
        ov = p.val("vs")
        p.waitf("scalar", "vs", ov)
        sq = p.op("scalar", lambda e, sl=sl: e.activation(out=gv("SQ64"), in_=gv("OH")[:, sl], func=AF.Square), inc="ss")
        cx.mm_into(2, [lambda e: e.matmul(out=pb[2][0:1, 0:512], lhsT=gv("ONESB")[0:64, 0:1],
                                          rhs=gv("SQ64"), start=True, stop=True)],
                   extra_waits=[("ss", sq)])
        cx.bank_read("scalar", 2, lambda e: e.activation(
            out=gv("RST1"), in_=pb[2][0:1, 0:512], func=AF.Ln, scale=1.0 / 64.0, bias=EPS_ON), "ss")
        p.selfw("scalar")
        rsv = p.op("scalar", lambda e: e.activation(out=gv("RST1B"), in_=gv("RST1"), func=AF.Exp, scale=-0.5), inc="ss")
        cx.mm_into(3, [lambda e: e.matmul(out=pb[3][0:64, 0:512], lhsT=gv("ONESB")[0:1, 0:64],
                                          rhs=gv("RST1B"), start=True, stop=True)],
                   extra_waits=[("ss", rsv)])
        cx.bank_read("vector", 3, lambda e, sl=sl: e.scalar_tensor_tensor(
            out=gv("ONS"), in0=gv("OH")[:, sl], scalar=gv("aonw")[0:64, 0:1],
            in1=pb[3][0:64, 0:512], op0=OP.mult, op1=OP.mult), "vs")
        p.waitf("scalar", "vs")
        siv = p.op("scalar", lambda e: e.activation(out=gv("ONS2"), in_=gv("ONS"), func=AF.Silu), inc="ss")
        p.wait("vector", "ss", siv)
        p.op("vector", lambda e, sl=sl: e.tensor_tensor(
            out=gv("OGS")[:, sl], in0=gv("GTld")[:, sl], in1=gv("ONS2"), op=OP.mult), inc="vs")
    ogv = p.val("vs")
    p.wait("scalar", "vs", ogv)
    p.dma("scalar", cx.SC["OGT"][b, h * 64:(h + 1) * 64, :], gv("OGS"), "dc")
    p.wait("vector", "dc")


def phase35(cx, b):
    p, gv, pb = cx.p, cx.gv, cx.pb
    cx.ws_rd["WS0"] = (p.val("ts"), p.val("vs"))
    ldv, ldsem = stream_weight(cx, "aWo", "WS0")
    p.wait("sync", "dc")
    for nt in range(NT):
        sl = slice(nt * 512, (nt + 1) * 512)
        p.wait("sync", "ts")
        p.wait("sync", "vs")
        p.wait("sync", "ss")
        lo = p.dma("sync", gv("OGL"), cx.SC["OGT"][b, :, sl].rearrange("(f p) t -> p f t", p=128), "ds")
        lx = p.dma("sync", gv("X0L"), cx.SC["X0T"][b, :, sl].rearrange("(f p) t -> p f t", p=128), "ds")
        for ft in range(FTL):
            fns = [lambda e, kt=kt, ft=ft: e.matmul(
                out=pb[ft][:, 0:512], lhsT=gv("WS0")[:, kt, ft * 128:(ft + 1) * 128],
                rhs=gv("OGL")[:, kt, :], start=(kt == 0), stop=(kt == FTL - 1)) for kt in range(FTL)]
            cx.mm_into(ft, fns, extra_waits=[(ldsem, ldv), ("ds", lo)])
        p.wait("vector", "ds", lx)
        for ft in range(FTL):
            cx.bank_read("vector", ft, lambda e, ft=ft: e.tensor_tensor(
                out=gv("X1G")[:, ft, :], in0=pb[ft][:, 0:512], in1=gv("X0L")[:, ft, :], op=OP.add), "vs")
        if b == 0 and nt == 0 and "p35" in cx.taps:
            cx.bank_read("vector", 0, lambda e: e.tensor_copy(out=gv("TS1"), in_=pb[0][:, 0:512]), "vs")
            dv = p.val("vs")
            p.wait("sync", "vs", dv)
            p.dma("sync", cx.TAP["x1g0"], gv("X1G").rearrange("p a b -> p (a b)"), "od")
            p.dma("sync", cx.TAP["ogl0"], gv("OGL").rearrange("p a b -> p (a b)"), "od")
            p.dma("sync", cx.TAP["ps0"], gv("TS1"), "od")
            p.dma("sync", cx.TAP["x0l0"], gv("X0L").rearrange("p a b -> p (a b)"), "od")
            p.dma("sync", cx.TAP["ws00"], gv("WS0").rearrange("p a b -> p (a b)"), "od")
            p.wait("vector", "od")
            p.wait("tensor", "od")
        _ln_rows(cx, b, nt, mode="x1")
    mark_ws_used(cx, "WS0")


def _ln_rows(cx, b, nt, mode):
    p, gv, pb = cx.p, cx.gv, cx.pb
    sg = gv("X1G")
    xv = p.val("vs")
    p.wait("vector", "ts")
    for ft in range(FTL):
        p.op("vector", lambda e, ft=ft: e.tensor_copy(out=gv("LNC")[:, ft, :], in_=sg[:, ft, :]), inc="vs")
    cv = p.val("vs")
    p.wait("scalar", "vs", cv)
    for ft in range(FTL):
        p.op("scalar", lambda e, ft=ft: e.activation(out=gv("LNC")[:, FTL + ft, :], in_=gv("LNC")[:, ft, :], func=AF.Square), inc="ss")
    sqv = p.val("ss")
    fns = [lambda e, kt=kt: e.matmul(out=pb[0][0:1, 0:512], lhsT=gv("S1024"),
                                     rhs=gv("LNC")[:, kt, :], start=(kt == 0), stop=(kt == FTL - 1))
           for kt in range(FTL)]
    cx.mm_into(0, fns, extra_waits=[("vs", cv)])
    fns = [lambda e, kt=kt: e.matmul(out=pb[1][0:1, 0:512], lhsT=gv("S1024"),
                                     rhs=gv("LNC")[:, FTL + kt, :], start=(kt == 0), stop=(kt == FTL - 1))
           for kt in range(FTL)]
    cx.mm_into(1, fns, extra_waits=[("ss", sqv)])
    cx.bank_read("vector", 0, lambda e: e.tensor_copy(out=gv("MR"), in_=pb[0][0:1, 0:512]), "vs")
    p.selfw("vector")
    p.op("vector", lambda e: e.tensor_tensor(out=gv("VRr"), in0=gv("MR"), in1=gv("MR"), op=OP.mult), inc="vs")
    p.selfw("vector")
    cx.bank_read("vector", 1, lambda e: e.tensor_tensor(out=gv("VRr"), in0=pb[1][0:1, 0:512], in1=gv("VRr"), op=OP.subtract), "vs")
    vv = p.val("vs")
    p.waitf("scalar", "vs", vv)
    p.op("scalar", lambda e: e.activation(out=gv("RSr"), in_=gv("VRr"), func=AF.Ln, bias=EPS_LN), inc="ss")
    p.selfw("scalar")
    p.op("scalar", lambda e: e.activation(out=gv("RSr"), in_=gv("RSr"), func=AF.Exp, scale=-0.5), inc="ss")
    if mode == "x1":
        p.op("scalar", lambda e: e.activation(out=gv("RS2r"), in_=gv("VRr"), func=AF.Ln,
                                              scale=(1.0 + EPS_LN), bias=(EPS_LN * EPS_LN)), inc="ss")
        p.selfw("scalar")
        p.op("scalar", lambda e: e.activation(out=gv("RS2r"), in_=gv("RS2r"), func=AF.Exp, scale=-0.5), inc="ss")
    rsv = p.val("ss")
    cx.mm_into(2, [lambda e: e.matmul(out=pb[2][:, 0:512], lhsT=gv("ONESF")[0:1, :],
                                      rhs=gv("MR"), start=True, stop=True)],
               extra_waits=[("vs", vv)])
    cx.mm_into(3, [lambda e: e.matmul(out=pb[3][:, 0:512], lhsT=gv("ONESF")[0:1, :],
                                      rhs=gv("RSr"), start=True, stop=True)],
               extra_waits=[("ss", rsv)])
    if mode == "x1":
        cx.mm_into(4, [lambda e: e.matmul(out=pb[4][:, 0:512], lhsT=gv("ONESF")[0:1, :],
                                          rhs=gv("RS2r"), start=True, stop=True)])
    ga, bb_ = ("fln1g", "fln1b") if mode == "x1" else ("fln2g", "fln2b")
    for ft in range(FTL):
        cx.bank_read("vector", 2, lambda e, ft=ft: e.tensor_tensor(
            out=gv("TS1"), in0=sg[:, ft, :], in1=pb[2][:, 0:512], op=OP.subtract), "vs")
        cx.bank_read("vector", 3, lambda e: e.tensor_tensor(
            out=gv("H2S"), in0=gv("TS1"), in1=pb[3][:, 0:512], op=OP.mult), "vs")
        h2v = p.val("vs")
        if mode == "x1":
            p.wait("scalar", "vs", h2v)
            p.dma("scalar", cx.SC["H2"][b, ft * 128:(ft + 1) * 128, nt * 512:(nt + 1) * 512], gv("H2S"), "dc")
            cx.bank_read("vector", 4, lambda e: e.tensor_tensor(
                out=gv("TS2"), in0=gv("TS1"), in1=pb[4][:, 0:512], op=OP.mult), "vs")
            p.op("vector", lambda e, ft=ft, ga=ga, bb_=bb_: e.tensor_scalar(
                out=gv("MBF"), in0=gv("TS2"), scalar1=gv(ga)[:, ft:ft + 1],
                scalar2=gv(bb_)[:, ft:ft + 1], op0=OP.mult, op1=OP.add), inc="vs")
            hhv = p.val("vs")
            p.wait("scalar", "vs", hhv)
            p.dma("scalar", cx.SC["HH"][b, ft * 128:(ft + 1) * 128, nt * 512:(nt + 1) * 512], gv("MBF"), "dc")
            p.op("vector", lambda e, ft=ft: e.tensor_tensor(
                out=gv("TS2"), in0=gv("H2S"), in1=sg[:, ft, :], op=OP.add), inc="vs")
            rpv = p.val("vs")
            p.wait("scalar", "vs", rpv)
            p.dma("scalar", cx.SC["RP"][b, ft * 128:(ft + 1) * 128, nt * 512:(nt + 1) * 512], gv("TS2"), "dc")
            p.wait("vector", "dc")
        else:
            p.op("vector", lambda e, ft=ft, ga=ga, bb_=bb_: e.tensor_scalar(
                out=gv("MBF"), in0=gv("H2S"), scalar1=gv(ga)[:, ft:ft + 1],
                scalar2=gv(bb_)[:, ft:ft + 1], op0=OP.mult, op1=OP.add), inc="vs")
            mv = p.val("vs")
            p.wait("scalar", "vs", mv)
            p.dma("scalar", cx.SC["MT"][b, ft * 128:(ft + 1) * 128, nt * 512:(nt + 1) * 512], gv("MBF"), "dc")
            p.wait("vector", "dc")
            if b == 0 and nt == 3 and ft == 4 and "p4y" in cx.taps:
                dv = p.val("vs")
                p.wait("sync", "vs", dv)
                p.wait("sync", "ss")
                p.dma("sync", cx.TAP["yH2S"], gv("H2S"), "od")
                p.dma("sync", cx.TAP["yMBF"], gv("MBF"), "od")
                p.dma("sync", cx.TAP["yMR"], gv("MR"), "od")
                p.dma("sync", cx.TAP["yRS"], gv("RSr"), "od")
                p.dma("sync", cx.TAP["yVR"], gv("VRr"), "od")
                p.dma("sync", cx.TAP["yX1G"], gv("X1G").rearrange("p a b -> p (a b)"), "od")
                p.dma("sync", cx.TAP["yLNC"], gv("LNC").rearrange("p a b -> p (a b)"), "od")
                p.wait("vector", "od")
                p.wait("scalar", "od")
                p.wait("tensor", "od")


def phase4(cx, b):
    p, gv, pb = cx.p, cx.gv, cx.pb
    p.wait("sync", "ts")
    p.wait("sync", "vs")
    p.wait("sync", "ss")
    p.wait("sync", "dc")
    lh = p.dma("sync", gv("HREG"),
               cx.SC["HH"][b].rearrange("(f p) t -> p f t", p=128), "ds")
    cx.ws_rd["WS0"] = (p.val("ts"), p.val("vs"))
    cx.ws_rd["WS1"] = (p.val("ts"), p.val("vs"))
    for wname, dst, wbuf in [("fWq", "QT", "WS0"), ("fWk", "KT", "WS1"), ("fWv", "VT", "WS0")]:
        ldv, ldsem = stream_weight(cx, wname, wbuf)
        ws = gv(wbuf)
        for ft in range(FTL):
            p.wait("vector", "dc")
            for nt in range(NT):
                bank = nt % 2
                psv = pb[bank][:, 0:512]
                fns = [lambda e, kt=kt, nt=nt, ft=ft, psv=psv, ws=ws: e.matmul(
                    out=psv, lhsT=ws[:, kt, ft * 128:(ft + 1) * 128],
                    rhs=gv("HREG")[:, kt, nt * 512:(nt + 1) * 512],
                    start=(kt == 0), stop=(kt == FTL - 1)) for kt in range(FTL)]
                cx.mm_into(bank, fns, extra_waits=[(ldsem, ldv), ("ds", lh)])
                cx.bank_read("vector", bank, lambda e, nt=nt, psv=psv: e.tensor_copy(
                    out=gv("STG")[:, nt, :], in_=psv), "vs")
            sgv = p.val("vs")
            p.wait("scalar", "vs", sgv)
            p.dma("scalar", cx.SC[dst][b, ft * 128:(ft + 1) * 128, :],
                  gv("STG").rearrange("p a b -> p (a b)"), "dc")
        mark_ws_used(cx, wbuf)
    p.wait("vector", "dc")
    p.op("vector", lambda e: e.memset(gv("VH").rearrange("p a b -> p (a b)"), 0.0), inc="vs")
    p.op("vector", lambda e: e.memset(gv("VH")[:, :, 64:68], 1.0), inc="vs")
    vh0 = p.val("vs")
    for h in range(H):
        p.wait("sync", "ts")
        p.wait("sync", "vs")
        p.wait("sync", "ss")
        lq = p.dma("sync", gv("IQ"), cx.SC["QT"][b, h * 64:(h + 1) * 64, :], "ds")
        lk = p.dma("sync", gv("IK"), cx.SC["KT"][b, h * 64:(h + 1) * 64, :], "ds")
        lv_ = p.dma("sync", gv("IV"), cx.SC["VT"][b, h * 64:(h + 1) * 64, :], "ds")
        for kt in range(NCH):
            bank = 4 + kt % 2
            psv = pb[bank][:, 0:32].bitcast(BF)
            cx.mm_into(bank, [lambda e, kt=kt, psv=psv: e.transpose(
                out=psv, in_=gv("IV")[:, kt * 128:(kt + 1) * 128], identity=gv("IDBF")[0:64, 0:64])],
                extra_waits=[("ds", lv_), ("vs", vh0)])
            cx.bank_read("vector", bank, lambda e, kt=kt, psv=psv: e.tensor_copy(
                out=gv("VH")[:, kt, 0:64], in_=psv), "vs")
        vhv = p.val("vs")
        for qt in range(NT):
            qsl = slice(qt * 512, (qt + 1) * 512)
            for kt in range(NCH):
                bank = kt % 2
                cx.mm_into(bank, [lambda e, kt=kt, bank=bank, qsl=qsl: e.matmul(
                    out=pb[bank][:, 0:512], lhsT=gv("IK")[:, kt * 128:(kt + 1) * 128],
                    rhs=gv("IQ")[:, qsl], start=True, stop=True)],
                    extra_waits=[("ds", lk)] if kt == 0 else [])
                cx.bank_read("scalar", bank, lambda e, kt=kt, bank=bank: e.activation(
                    out=gv("LNC")[:, kt, :], in_=pb[bank][:, 0:512], func=AF.Exp, scale=0.125), "ss")
            expv = p.val("ss")
            fns = [lambda e, kt=kt: e.matmul(
                out=pb[6][0:68, 0:512], lhsT=gv("VH")[:, kt, 0:68], rhs=gv("LNC")[:, kt, :],
                start=(kt == 0), stop=(kt == NCH - 1)) for kt in range(NCH)]
            cx.mm_into(6, fns, extra_waits=[("ss", expv), ("vs", vhv)])
            cx.bank_read("vector", 6, lambda e: e.reciprocal(out=gv("RECRF")[64:65, :], in_=pb[6][64:65, 0:512]), "vs")
            p.selfw("vector")
            rv = p.op("vector", lambda e: e.tensor_copy(out=gv("RECBF")[64:65, :], in_=gv("RECRF")[64:65, :]), inc="vs")
            cx.mm_into(7, [lambda e: e.matmul(out=pb[7][0:64, 0:512], lhsT=gv("ONESB")[64:65, 0:64],
                                              rhs=gv("RECBF")[64:65, :], start=True, stop=True)],
                       extra_waits=[("vs", rv)])
            cx.bank_read("vector", 7, lambda e: e.tensor_copy(out=gv("RC64"), in_=pb[7][0:64, 0:512]), "vs")
            cx.bank_read("vector", 6, lambda e: e.tensor_tensor(
                out=gv("CTB"), in0=pb[6][0:64, 0:512], in1=gv("RC64"), op=OP.mult), "vs")
            cxv = p.val("vs")
            p.wait("scalar", "vs", cxv)
            p.dma("scalar", cx.SC["QT"][b, h * 64:(h + 1) * 64, qsl], gv("CTB"), "dc")
            p.wait("vector", "dc")
    ldv, ldsem = stream_weight(cx, "fWo", "WS0")
    p.wait("sync", "dc")
    for nt in range(NT):
        sl = slice(nt * 512, (nt + 1) * 512)
        p.wait("sync", "ts")
        p.wait("sync", "vs")
        p.wait("sync", "ss")
        lo = p.dma("sync", gv("OGL"), cx.SC["QT"][b, :, sl].rearrange("(f p) t -> p f t", p=128), "ds")
        lh2 = p.dma("sync", gv("X0L"), cx.SC["H2"][b, :, sl].rearrange("(f p) t -> p f t", p=128), "ds")
        lrp = p.dma("sync", gv("RLD"), cx.SC["RP"][b, :, sl].rearrange("(f p) t -> p f t", p=128), "ds")
        for ft in range(FTL):
            fns = [lambda e, kt=kt, ft=ft: e.matmul(
                out=pb[ft][:, 0:512], lhsT=gv("WS0")[:, kt, ft * 128:(ft + 1) * 128],
                rhs=gv("OGL")[:, kt, :], start=(kt == 0), stop=(kt == FTL - 1)) for kt in range(FTL)]
            cx.mm_into(ft, fns, extra_waits=[(ldsem, ldv), ("ds", lo)])
        p.wait("vector", "ds", lh2)
        p.wait("vector", "ds", lrp)
        for ft in range(FTL):
            cx.bank_read("vector", ft, lambda e, ft=ft: e.tensor_tensor(
                out=gv("X1G")[:, ft, :], in0=pb[ft][:, 0:512], in1=gv("X0L")[:, ft, :], op=OP.add), "vs")
            cx.bank_read("vector", ft, lambda e, ft=ft: e.tensor_tensor(
                out=gv("TS1"), in0=pb[ft][:, 0:512], in1=gv("RLD")[:, ft, :], op=OP.add), "vs")
            rrv = p.val("vs")
            p.wait("scalar", "vs", rrv)
            p.dma("scalar", cx.SC["RR"][b, ft * 128:(ft + 1) * 128, sl], gv("TS1"), "dc")
            p.wait("vector", "dc")
        _ln_rows(cx, b, nt, mode="y")
    mark_ws_used(cx, "WS0")


def phase5(cx, nB, skip_out=False, skip_experts=False, dummy_out=False):
    p, gv, pb = cx.p, cx.gv, cx.pb
    p.wait("sync", "dw", cx.w12_casts)
    p.wait("sync", "ts")
    p.wait("sync", "vs")
    p.wait("sync", "ss")
    p.wait("sync", "dc")
    cx.ws_rd["WGS"] = (p.val("ts"), p.val("vs"))
    ldwg, _ = stream_weight(cx, "fWg", "WGS")
    wreg = gv("WREG")
    R0 = gv("WR0Q").rearrange("p a b -> p (a b)").rearrange("p (a b) -> p a b", b=2048)
    R1 = gv("WR1Q").rearrange("p a b -> p (a b)").rearrange("p (a b) -> p a b", b=2048)
    r_rd = [(p.val("ts"), p.val("ss")), (p.val("ts"), p.val("ss"))]
    for w in range(nB * NT):
        b, nt = w // NT, w % NT
        sl = slice(nt * 512, (nt + 1) * 512)
        p.wait("sync", "ts")
        p.wait("sync", "vs")
        p.wait("sync", "ss")
        p.wait("sync", "dc")
        lm = p.dma("sync", gv("MW"), cx.SC["MT"][b, :, sl].rearrange("(f p) t -> p f t", p=128), "ds")
        lr = p.dma("sync", gv("RL5"), cx.SC["RR"][b, :, sl].rearrange("(f p) t -> p f t", p=128), "ds")
        for st in range(4):
            ssl = slice(st * 128, (st + 1) * 128)
            fns = [lambda e, kt=kt, ssl=ssl: e.matmul(
                out=pb[4][0:128, 0:8], lhsT=gv("MW")[:, kt, ssl], rhs=gv("WGS")[:, kt, :],
                start=(kt == 0), stop=(kt == FTL - 1)) for kt in range(FTL)]
            cx.mm_into(4, fns, extra_waits=[("ds", max(lm, ldwg))])
            cx.bank_read("vector", 4, lambda e: e.reduce_max(out=gv("GCOL")[:, 0:1], in_=pb[4][0:128, 0:8], axis=AX.X), "vs")
            p.selfw("vector")
            p.op("vector", lambda e: e.tensor_scalar(out=gv("GCOL")[:, 1:2], in0=gv("GCOL")[:, 0:1], scalar1=-1.0, scalar2=None, op0=OP.mult), inc="vs")
            nv = p.val("vs")
            cx.bank_read("scalar", 4, lambda e: e.activation(
                out=gv("PRB"), in_=pb[4][0:128, 0:8], func=AF.Exp, bias=gv("GCOL")[:, 1:2]), "ss",
                extra_waits=[("vs", nv)])
            p.wait("vector", "ss")
            p.op("vector", lambda e: e.reduce_max(out=gv("GCOL")[:, 2:3], in_=gv("PRB"), axis=AX.X), inc="vs")
            p.selfw("vector")
            p.op("vector", lambda e: e.tensor_scalar(out=gv("GSB"), in0=gv("PRB"), scalar1=gv("GCOL")[:, 2:3], scalar2=None, op0=OP.is_ge), inc="vs")
            p.selfw("vector")
            p.op("vector", lambda e: e.scalar_tensor_tensor(out=gv("GSB2"), in0=gv("GSB"), scalar=-2.0, in1=gv("PRB"), op0=OP.mult, op1=OP.add), inc="vs")
            p.selfw("vector")
            p.op("vector", lambda e: e.reduce_max(out=gv("GCOL")[:, 3:4], in_=gv("GSB2"), axis=AX.X), inc="vs")
            p.selfw("vector")
            p.op("vector", lambda e: e.tensor_scalar(out=gv("GSB"), in0=gv("PRB"), scalar1=gv("GCOL")[:, 3:4], scalar2=None, op0=OP.is_ge), inc="vs")
            p.selfw("vector")
            p.op("vector", lambda e: e.tensor_mul(out=gv("GSB"), in0=gv("GSB"), in1=gv("PRB")), inc="vs")
            p.selfw("vector")
            p.op("vector", lambda e: e.reduce_sum(out=gv("GCOL")[:, 4:5], in_=gv("GSB"), axis=AX.X), inc="vs")
            p.selfw("vector")
            p.op("vector", lambda e: e.reciprocal(out=gv("GCOL")[:, 5:6], in_=gv("GCOL")[:, 4:5]), inc="vs")
            p.selfw("vector")
            p.op("vector", lambda e, st=st: e.tensor_scalar(out=gv("G4")[:, st, :], in0=gv("GSB"), scalar1=gv("GCOL")[:, 5:6], scalar2=None, op0=OP.mult), inc="vs")
        gvv = p.val("vs")
        for e_ in range(NE):
            fns = [lambda e, st=st, e_=e_: e.matmul(
                out=pb[5][0:1, st * 128:(st + 1) * 128], lhsT=gv("G4")[:, st, e_:e_ + 1],
                rhs=gv("IDF"), start=True, stop=True) for st in range(4)]
            cx.mm_into(5, fns, extra_waits=[("vs", gvv)])
            cx.bank_read("vector", 5, lambda e: e.tensor_copy(out=gv("GRE"), in_=pb[5][0:1, 0:512]), "vs")
            gre = p.val("vs")
            cx.mm_into(5, [lambda e: e.matmul(
                out=pb[5][:, 0:512], lhsT=gv("ONESF")[0:1, :],
                rhs=gv("GRE"), start=True, stop=True)], extra_waits=[("vs", gre)])
            cx.bank_read("vector", 5, lambda e, e_=e_: e.tensor_copy(
                out=gv("GBCA")[:, e_, :], in_=pb[5][:, 0:512]), "vs")
        p.wait("vector", "ds", lr)
        p.wait("vector", "ts")
        for ft in range(FTL):
            p.op("vector", lambda e, ft=ft: e.tensor_copy(out=gv("ACC")[:, ft, :], in_=gv("RL5")[:, ft, :]), inc="vs")
        for e_ in range(NE if not skip_experts else 0):
            for half, reg, ri in [(0, R0, 0), (1, R1, 1)]:
                t_, s_ = r_rd[ri]
                p.wait("sync", "ts", t_)
                p.wait("sync", "ss", s_)
                lw = p.dma("sync", reg, cx.WB["fW1"][e_][:, half * 2048:(half + 1) * 2048]
                           .rearrange("(k p) f -> p k f", p=128), "ds")
                for fft in range(16):
                    bank = fft % 4
                    fns = [lambda e, kt=kt, fft=fft, bank=bank, reg=reg: e.matmul(
                        out=pb[bank][:, 0:512], lhsT=reg[:, kt, fft * 128:(fft + 1) * 128],
                        rhs=gv("MW")[:, kt, :], start=(kt == 0), stop=(kt == FTL - 1))
                        for kt in range(FTL)]
                    cx.mm_into(bank, fns, extra_waits=[("ds", lw)] if fft == 0 else [])
                    cx.bank_read("scalar", bank, lambda e, fft=fft, half=half, bank=bank, e_=e_: e.activation(
                        out=gv("HB1")[:, half * 16 + fft, :], in_=pb[bank][:, 0:512],
                        func=AF.Gelu_apprx_tanh, bias=gv("fb1")[:, e_, half * 16 + fft:half * 16 + fft + 1]), "ss")
                r_rd[ri] = (p.val("ts"), p.val("ss"))
            h1v = p.val("ss")
            R0q = gv("WR0Q")
            R1q = gv("WR1Q")
            for pair, rq, ri in [(0, R0q, 0), (1, R1q, 1)]:
                t_, s_ = r_rd[ri]
                p.wait("sync", "ts", t_)
                p.wait("sync", "ss", s_)
                lw2 = p.dma("sync", rq, cx.WB["fW2"][e_][pair * 2048:(pair + 1) * 2048, :]
                            .rearrange("(k p) m -> p k m", p=128), "ds")
                for qq in range(2):
                    q = pair * 2 + qq
                    for ftout in range(FTL):
                        fns = [lambda e, kt=kt, ftout=ftout, q=q, qq=qq, rq=rq: e.matmul(
                            out=pb[ftout][:, 0:512], lhsT=rq[:, qq * 8 + kt, ftout * 128:(ftout + 1) * 128],
                            rhs=gv("HB1")[:, q * 8 + kt, :],
                            start=(q == 0 and kt == 0), stop=(q == 3 and kt == 7))
                            for kt in range(8)]
                        ew = [("ds", lw2), ("ss", h1v)] if (qq == 0 and ftout == 0) else []
                        cx.mm_into(ftout, fns, extra_waits=ew)
                r_rd[ri] = (p.val("ts"), p.val("ss"))
            for ftout in range(FTL):
                cx.bank_read("vector", ftout, lambda e, ftout=ftout, e_=e_: e.scalar_tensor_tensor(
                    out=gv("TS5"), in0=pb[ftout][:, 0:512], scalar=gv("fb2")[:, e_, ftout:ftout + 1],
                    in1=gv("GBCA")[:, e_, :], op0=OP.add, op1=OP.mult), "vs")
                p.op("vector", lambda e, ftout=ftout: e.tensor_tensor(
                    out=gv("ACC")[:, ftout, :], in0=gv("ACC")[:, ftout, :], in1=gv("TS5"), op=OP.add), inc="vs")
        accv = p.val("vs")
        if skip_out:
            continue
        for st in range(4):
            for ft in range(FTL):
                bank = ft % 2
                cx.mm_into(bank, [lambda e, ft=ft, st=st, bank=bank: e.transpose(
                    out=pb[bank][:, 0:128], in_=gv("ACC")[:, ft, st * 128:(st + 1) * 128],
                    identity=gv("IDF"))], extra_waits=[("vs", accv)])
                cx.bank_read("vector", bank, lambda e, ft=ft, bank=bank: e.tensor_copy(
                    out=gv("OST")[:, ft * 128:(ft + 1) * 128], in_=pb[bank][:, 0:128]), "vs")
            ov = p.val("vs")
            p.wait("sync", "vs", ov)
            tgt = cx.outdummy if dummy_out else cx.outstage
            p.dma("sync", tgt[b, nt * 512 + st * 128: nt * 512 + (st + 1) * 128, :], gv("OST"), "dc")
            p.wait("vector", "dc")


def kernel(**inputs):
    ncores = 8
    x = np.ascontiguousarray(inputs["L0_attn_x"])
    shards = np.split(x, ncores, axis=0)
    cst = host_constants()
    base = {}
    base["aWq"] = inputs["L0_attn_Wq"]; base["aWk"] = inputs["L0_attn_Wk"]
    base["aWv"] = inputs["L0_attn_Wv"]; base["aWg"] = inputs["L0_attn_Wg"]
    base["aWo"] = inputs["L0_attn_Wo"]; base["abw"] = inputs["L0_attn_bw"]
    base["agkw"] = inputs["L0_attn_gkw"]
    base["fWq"] = inputs["L0_ffn_Wq"]; base["fWk"] = inputs["L0_ffn_Wk"]
    base["fWv"] = inputs["L0_ffn_Wv"]; base["fWo"] = inputs["L0_ffn_Wo"]
    base["fWg"] = inputs["L0_ffn_Wg"]; base["fW1"] = inputs["L0_ffn_W1"]
    base["fW2"] = inputs["L0_ffn_W2"]
    base["abb"] = inputs["L0_attn_bb"].reshape(16, 1)
    base["agkb"] = inputs["L0_attn_gkb"].reshape(16, 1)
    base["aAlog"] = inputs["L0_attn_A_log"].reshape(16, 1)
    base["adtb"] = inputs["L0_attn_dt_bias"].reshape(16, 1)
    base["aD"] = inputs["L0_attn_D_param"].reshape(1, 16)
    base["aonw"] = np.tile(inputs["L0_attn_onorm_w"].reshape(DH, 1), (2, 1))
    def pcol(v):
        return np.ascontiguousarray(v.reshape(FTL, 128).T)
    def pcol3(v):
        return np.ascontiguousarray(v.reshape(FTL, 128, -1).transpose(1, 0, 2))
    base["aqcw"] = pcol3(inputs["L0_attn_qcw"].reshape(D, 4))
    base["aqcb"] = pcol(inputs["L0_attn_qcb"])
    base["akcw"] = pcol3(inputs["L0_attn_kcw"].reshape(D, 4))
    base["akcb"] = pcol(inputs["L0_attn_kcb"])
    base["avcw"] = pcol3(inputs["L0_attn_vcw"].reshape(D, 4))
    base["avcb"] = pcol(inputs["L0_attn_vcb"])
    base["fln1g"] = pcol(inputs["L0_ffn_ln1_g"]); base["fln1b"] = pcol(inputs["L0_ffn_ln1_b"])
    base["fln2g"] = pcol(inputs["L0_ffn_ln2_g"]); base["fln2b"] = pcol(inputs["L0_ffn_ln2_b"])
    base["fb1"] = np.ascontiguousarray(inputs["L0_ffn_b1"].reshape(NE, FF // 128, 128).transpose(2, 0, 1))
    base["fb2"] = np.ascontiguousarray(inputs["L0_ffn_b2"].reshape(NE, FTL, 128).transpose(2, 0, 1))
    for nm, a in cst.items():
        base["c_" + nm] = a
    nc, _ = build(nB=B, taps=())
    in_maps = []
    for s in shards:
        m = dict(base)
        m["x"] = s
        in_maps.append(m)
    res = run_bass_kernel_spmd(nc, in_maps, core_ids=list(range(ncores)))
    return np.concatenate([r["out"] for r in res.results], axis=0)
